# version 01 (final)
# baseline (speedup 1.0000x reference)
import numpy as np
import ml_dtypes
from contextlib import ExitStack
import concourse.bass as bass
import concourse.mybir as mybir
from concourse.bass_utils import run_bass_kernel_spmd

F32 = mybir.dt.float32
BF16 = mybir.dt.bfloat16
I32 = mybir.dt.int32
AF = mybir.ActivationFunctionType
ALU = mybir.AluOpType
AX = mybir.AxisListType
NCORE = 8
BF = ml_dtypes.bfloat16


class Em:
    def __init__(self):
        self.ops = []

    def c(self, eng, fn):
        self.ops.append((eng, 'c', fn))

    def d(self, fn):
        self.ops.append(('sync', 'd', fn))

    def finalize(self, nc, C, D):
        pre = []
        ct, dt = 0, 0
        for eng, kind, fn in self.ops:
            pre.append((ct, dt))
            if kind == 'c':
                ct += 1
            else:
                dt += 16
        ctot, dtot = ct, dt
        ops = self.ops

        def mk(name, is_sync):
            def sec(eng):
                lastc, lastd = 0, 0
                for i, (en, kind, fn) in enumerate(ops):
                    if en != name:
                        continue
                    cb, db = pre[i]
                    if cb > lastc:
                        eng.wait_ge(C, cb)
                        lastc = cb
                    if db > lastd:
                        eng.wait_ge(D, db)
                        lastd = db
                    instr = fn(eng)
                    if kind == 'c':
                        instr.then_inc(C, 1)
                    else:
                        instr.then_inc(D, 16)
                if is_sync:
                    if ctot > 0:
                        eng.wait_ge(C, ctot)
                    if dtot > 0:
                        eng.wait_ge(D, dtot)
            return sec

        with nc.Block() as block:
            block.sync(mk('sync', True))
            block.tensor(mk('tensor', False))
            block.vector(mk('vector', False))
            block.scalar(mk('scalar', False))


def _bass():
    return bass.Bass("TRN2", target_bir_lowering=False)


def build_ln():
    nc = _bass()
    x = nc.dram_tensor("x", [2048, 2048], F32, kind="ExternalInput").ap()
    y = nc.dram_tensor("y", [2048, 2048], F32, kind="ExternalOutput").ap()
    with ExitStack() as st:
        C = st.enter_context(nc.semaphore("C"))
        D = st.enter_context(nc.semaphore("D"))
        xt = st.enter_context(nc.sbuf_tensor("xt", [128, 2048], F32))
        sq = st.enter_context(nc.sbuf_tensor("sq", [128, 2048], F32))
        yt = st.enter_context(nc.sbuf_tensor("yt", [128, 2048], F32))
        m = st.enter_context(nc.sbuf_tensor("m", [128, 1], F32))
        s2 = st.enter_context(nc.sbuf_tensor("s2", [128, 1], F32))
        var = st.enter_context(nc.sbuf_tensor("var", [128, 1], F32))
        rstd = st.enter_context(nc.sbuf_tensor("rstd", [128, 1], F32))
        nm = st.enter_context(nc.sbuf_tensor("nm", [128, 1], F32))
        E = Em()
        for t in range(16):
            r0 = t * 128
            E.d(lambda e, r0=r0: e.dma_start(out=xt[:], in_=x[r0:r0 + 128, :]))
            E.c('vector', lambda e: e.reduce_sum(m[:], xt[:], axis=AX.X))
            E.c('vector', lambda e: e.tensor_scalar_mul(m[:], m[:], 1.0 / 2048.0))
            E.c('scalar', lambda e: e.activation(sq[:], xt[:], AF.Square))
            E.c('vector', lambda e: e.reduce_sum(s2[:], sq[:], axis=AX.X))
            E.c('vector', lambda e: e.tensor_scalar_mul(s2[:], s2[:], 1.0 / 2048.0))
            E.c('vector', lambda e: e.tensor_mul(var[:], m[:], m[:]))
            E.c('vector', lambda e: e.tensor_sub(var[:], s2[:], var[:]))
            E.c('vector', lambda e: e.tensor_scalar(var[:], var[:], scalar1=1e-5, scalar2=1.0, op0=ALU.add, op1=ALU.mult))
            E.c('scalar', lambda e: e.activation(var[:], var[:], AF.Sqrt))
            E.c('vector', lambda e: e.reciprocal(rstd[:], var[:]))
            E.c('vector', lambda e: e.tensor_mul(nm[:], m[:], rstd[:]))
            E.c('vector', lambda e: e.tensor_scalar_mul(nm[:], nm[:], -1.0))
            E.c('vector', lambda e: e.tensor_scalar(yt[:], xt[:], scalar1=rstd[:], scalar2=nm[:], op0=ALU.mult, op1=ALU.add))
            E.d(lambda e, r0=r0: e.dma_start(out=y[r0:r0 + 128, :], in_=yt[:]))
        E.finalize(nc, C, D)
    return nc


def build_mm(O, name):
    nc = _bass()
    xT = nc.dram_tensor("xT", [2048, 2048], F32, kind="ExternalInput").ap()
    WT = nc.dram_tensor("WT", [2048, O], F32, kind="ExternalInput").ap()
    y = nc.dram_tensor("y", [2048, O], F32, kind="ExternalOutput").ap()
    ocs = []
    o0 = 0
    while o0 < O:
        w = min(512, O - o0)
        ocs.append((o0, w))
        o0 += w
    with ExitStack() as st:
        C = st.enter_context(nc.semaphore("C"))
        D = st.enter_context(nc.semaphore("D"))
        xst = st.enter_context(nc.sbuf_tensor("xst", [128, 2048], F32))
        xbf = st.enter_context(nc.sbuf_tensor("xbf", [128, 16, 2048], BF16))
        wst = st.enter_context(nc.sbuf_tensor("wst", [128, 512], F32))
        wbf = st.enter_context(nc.sbuf_tensor("wbf", [128, 16, 512], BF16))
        yb = st.enter_context(nc.sbuf_tensor("yb", [128, 512], F32))
        ps = st.enter_context(nc.psum_tensor("ps", [128, 512], F32))
        E = Em()
        for dc in range(16):
            E.d(lambda e, dc=dc: e.dma_start(out=xst[:], in_=xT[dc * 128:dc * 128 + 128, :]))
            E.c('vector', lambda e, dc=dc: e.tensor_copy(xbf[:, dc, :], xst[:]))
        for (o0, w) in ocs:
            for dc in range(16):
                E.d(lambda e, dc=dc, o0=o0, w=w: e.dma_start(out=wst[:, :w], in_=WT[dc * 128:dc * 128 + 128, o0:o0 + w]))
                E.c('vector', lambda e, dc=dc, w=w: e.tensor_copy(wbf[:, dc, :w], wst[:, :w]))
            for tt in range(16):
                for dc in range(16):
                    E.c('tensor', lambda e, dc=dc, tt=tt, w=w: e.matmul(
                        ps[:, :w], xbf[:, dc, tt * 128:tt * 128 + 128], wbf[:, dc, :w],
                        start=(dc == 0), stop=(dc == 15)))
                E.c('vector', lambda e, w=w: e.tensor_copy(yb[:, :w], ps[:, :w]))
                E.d(lambda e, tt=tt, o0=o0, w=w: e.dma_start(out=y[tt * 128:tt * 128 + 128, o0:o0 + w], in_=yb[:, :w]))
        E.finalize(nc, C, D)
    return nc


def build_qmm(K, O, name):
    dcn = K // 128
    ocn = O // 128
    nc = _bass()
    xT = nc.dram_tensor("xT", [K, 2048], F32, kind="ExternalInput").ap()
    WT = nc.dram_tensor("WT", [K, O], I32, kind="ExternalInput").ap()
    sc = nc.dram_tensor("sc", [O, 1], F32, kind="ExternalInput").ap()
    y = nc.dram_tensor("y", [O, 2048], F32, kind="ExternalOutput").ap()
    with ExitStack() as st:
        C = st.enter_context(nc.semaphore("C"))
        D = st.enter_context(nc.semaphore("D"))
        xst = st.enter_context(nc.sbuf_tensor("xst", [128, 2048], F32))
        xbf = st.enter_context(nc.sbuf_tensor("xbf", [128, dcn, 2048], BF16))
        wsti = st.enter_context(nc.sbuf_tensor("wsti", [128, 128], I32))
        wstf = st.enter_context(nc.sbuf_tensor("wstf", [128, 128], F32))
        wbf = st.enter_context(nc.sbuf_tensor("wbf", [128, dcn, 128], BF16))
        scol = st.enter_context(nc.sbuf_tensor("scol", [128, 1], F32))
        yb = st.enter_context(nc.sbuf_tensor("yb", [128, 512], F32))
        ps = st.enter_context(nc.psum_tensor("ps", [128, 512], F32))
        E = Em()
        for dc in range(dcn):
            E.d(lambda e, dc=dc: e.dma_start(out=xst[:], in_=xT[dc * 128:dc * 128 + 128, :]))
            E.c('vector', lambda e, dc=dc: e.tensor_copy(xbf[:, dc, :], xst[:]))
        for oc in range(ocn):
            E.d(lambda e, oc=oc: e.dma_start(out=scol[:], in_=sc[oc * 128:oc * 128 + 128, :]))
            for dc in range(dcn):
                E.d(lambda e, dc=dc, oc=oc: e.dma_start(out=wsti[:], in_=WT[dc * 128:dc * 128 + 128, oc * 128:oc * 128 + 128]))
                E.c('vector', lambda e: e.tensor_copy(wstf[:], wsti[:]))
                E.c('vector', lambda e, dc=dc: e.tensor_scalar(wbf[:, dc, :], wstf[:], scalar1=-128.0, scalar2=1.0, op0=ALU.add, op1=ALU.mult))
            for tt in range(4):
                for dc in range(dcn):
                    E.c('tensor', lambda e, dc=dc, tt=tt: e.matmul(
                        ps[:], wbf[:, dc, :], xbf[:, dc, tt * 512:tt * 512 + 512],
                        start=(dc == 0), stop=(dc == dcn - 1)))
                E.c('vector', lambda e: e.tensor_scalar_mul(yb[:], ps[:], scol[:]))
                E.d(lambda e, oc=oc, tt=tt: e.dma_start(out=y[oc * 128:oc * 128 + 128, tt * 512:tt * 512 + 512], in_=yb[:]))
        E.finalize(nc, C, D)
    return nc


def build_conv():
    nc = _bass()
    xp = nc.dram_tensor("xp", [768, 2, 8195], F32, kind="ExternalInput").ap()
    cw = nc.dram_tensor("cw", [4, 768, 1], F32, kind="ExternalInput").ap()
    cb = nc.dram_tensor("cb", [768, 1], F32, kind="ExternalInput").ap()
    y = nc.dram_tensor("y", [768, 2, 8192], F32, kind="ExternalOutput").ap()
    with ExitStack() as st:
        C = st.enter_context(nc.semaphore("C"))
        D = st.enter_context(nc.semaphore("D"))
        xt = st.enter_context(nc.sbuf_tensor("xt", [128, 2051], F32))
        acc = st.enter_context(nc.sbuf_tensor("acc", [128, 2048], F32))
        tmp = st.enter_context(nc.sbuf_tensor("tmp", [128, 2048], F32))
        wc = st.enter_context(nc.sbuf_tensor("wc", [128, 4], F32))
        bc = st.enter_context(nc.sbuf_tensor("bc", [128, 1], F32))
        E = Em()
        for pc in range(6):
            r0 = pc * 128
            for j in range(4):
                E.d(lambda e, j=j, r0=r0: e.dma_start(out=wc[:, j:j + 1], in_=cw[j, r0:r0 + 128, :]))
            E.d(lambda e, r0=r0: e.dma_start(out=bc[:], in_=cb[r0:r0 + 128, :]))
            for b in range(2):
                for ft in range(4):
                    c0 = ft * 2048
                    E.d(lambda e, r0=r0, b=b, c0=c0: e.dma_start(out=xt[:], in_=xp[r0:r0 + 128, b, c0:c0 + 2051]))
                    E.c('vector', lambda e: e.tensor_scalar_mul(acc[:], xt[:, 0:2048], wc[:, 0:1]))
                    for j in range(1, 4):
                        E.c('vector', lambda e, j=j: e.tensor_scalar_mul(tmp[:], xt[:, j:j + 2048], wc[:, j:j + 1]))
                        E.c('vector', lambda e: e.tensor_add(acc[:], acc[:], tmp[:]))
                    E.c('vector', lambda e: e.tensor_scalar(acc[:], acc[:], scalar1=bc[:], scalar2=1.0, op0=ALU.add, op1=ALU.mult))
                    E.c('scalar', lambda e: e.activation(acc[:], acc[:], AF.Silu))
                    E.d(lambda e, r0=r0, b=b, c0=c0: e.dma_start(out=y[r0:r0 + 128, b, c0:c0 + 2048], in_=acc[:]))
        E.finalize(nc, C, D)
    return nc


def build_decay():
    nc = _bass()
    aT = nc.dram_tensor("aT", [16, 2048], F32, kind="ExternalInput").ap()
    bT = nc.dram_tensor("bT", [16, 2048], F32, kind="ExternalInput").ap()
    Al = nc.dram_tensor("Al", [16, 1], F32, kind="ExternalInput").ap()
    dtb = nc.dram_tensor("dtb", [16, 1], F32, kind="ExternalInput").ap()
    lg = nc.dram_tensor("lg", [16, 2048], F32, kind="ExternalOutput").ap()
    bt = nc.dram_tensor("bt", [16, 2048], F32, kind="ExternalOutput").ap()
    with ExitStack() as st:
        C = st.enter_context(nc.semaphore("C"))
        D = st.enter_context(nc.semaphore("D"))
        at = st.enter_context(nc.sbuf_tensor("at", [16, 2048], F32))
        bb = st.enter_context(nc.sbuf_tensor("bb", [16, 2048], F32))
        t0 = st.enter_context(nc.sbuf_tensor("t0", [16, 2048], F32))
        ab = st.enter_context(nc.sbuf_tensor("ab", [16, 2048], F32))
        ee = st.enter_context(nc.sbuf_tensor("ee", [16, 2048], F32))
        rl = st.enter_context(nc.sbuf_tensor("rl", [16, 2048], F32))
        alc = st.enter_context(nc.sbuf_tensor("alc", [16, 1], F32))
        dtc = st.enter_context(nc.sbuf_tensor("dtc", [16, 1], F32))
        nea = st.enter_context(nc.sbuf_tensor("nea", [16, 1], F32))
        E = Em()
        E.d(lambda e: e.dma_start(out=at[:], in_=aT[:, :]))
        E.d(lambda e: e.dma_start(out=bb[:], in_=bT[:, :]))
        E.d(lambda e: e.dma_start(out=alc[:], in_=Al[:, :]))
        E.d(lambda e: e.dma_start(out=dtc[:], in_=dtb[:, :]))
        E.c('vector', lambda e: e.tensor_scalar(t0[:], at[:], scalar1=dtc[:], scalar2=1.0, op0=ALU.add, op1=ALU.mult))
        E.c('scalar', lambda e: e.activation(ab[:], t0[:], AF.Abs))
        E.c('vector', lambda e: e.tensor_scalar_mul(ab[:], ab[:], -1.0))
        E.c('scalar', lambda e: e.activation(ee[:], ab[:], AF.Exp))
        E.c('vector', lambda e: e.tensor_scalar(ee[:], ee[:], scalar1=1.0, scalar2=1.0, op0=ALU.add, op1=ALU.mult))
        E.c('scalar', lambda e: e.activation(ee[:], ee[:], AF.Ln))
        E.c('vector', lambda e: e.tensor_scalar(rl[:], t0[:], scalar1=0.0, scalar2=1.0, op0=ALU.max, op1=ALU.mult))
        E.c('vector', lambda e: e.tensor_add(rl[:], rl[:], ee[:]))
        E.c('scalar', lambda e: e.activation(nea[:], alc[:], AF.Exp))
        E.c('vector', lambda e: e.tensor_scalar_mul(nea[:], nea[:], -1.0))
        E.c('vector', lambda e: e.tensor_scalar_mul(rl[:], rl[:], nea[:]))
        E.d(lambda e: e.dma_start(out=lg[:, :], in_=rl[:]))
        E.c('scalar', lambda e: e.activation(bb[:], bb[:], AF.Sigmoid))
        E.d(lambda e: e.dma_start(out=bt[:, :], in_=bb[:]))
        E.finalize(nc, C, D)
    return nc


def build_scan():
    nc = _bass()
    qch = nc.dram_tensor("qch", [4, 128, 1024], F32, kind="ExternalInput").ap()
    kch = nc.dram_tensor("kch", [4, 128, 1024], F32, kind="ExternalInput").ap()
    ktk = nc.dram_tensor("ktk", [4, 8, 128, 128], F32, kind="ExternalInput").ap()
    vtk = nc.dram_tensor("vtk", [4, 8, 128, 128], F32, kind="ExternalInput").ap()
    lgc = nc.dram_tensor("lgc", [4, 8, 128, 1], F32, kind="ExternalInput").ap()
    btr = nc.dram_tensor("btr", [4, 8, 1, 128], F32, kind="ExternalInput").ap()
    btc = nc.dram_tensor("btc", [4, 8, 128, 1], F32, kind="ExternalInput").ap()
    Sin = nc.dram_tensor("Sin", [4, 128, 128], F32, kind="ExternalInput").ap()
    triu = nc.dram_tensor("triu", [128, 128], F32, kind="ExternalInput").ap()
    idf = nc.dram_tensor("idf", [128, 128], F32, kind="ExternalInput").ap()
    idb = nc.dram_tensor("idb", [128, 128], BF16, kind="ExternalInput").ap()
    mS = nc.dram_tensor("mS", [128, 128], F32, kind="ExternalInput").ap()
    mI = nc.dram_tensor("mI", [128, 128], F32, kind="ExternalInput").ap()
    onr = nc.dram_tensor("onr", [1, 128], F32, kind="ExternalInput").ap()
    onc = nc.dram_tensor("onc", [128, 1], F32, kind="ExternalInput").ap()
    oh = nc.dram_tensor("oh", [128, 1], F32, kind="ExternalInput").ap()
    oo = nc.dram_tensor("oo", [4, 8, 128, 128], F32, kind="ExternalOutput").ap()
    So = nc.dram_tensor("So", [4, 128, 128], F32, kind="ExternalOutput").ap()
    with ExitStack() as st:
        C = st.enter_context(nc.semaphore("C"))
        D = st.enter_context(nc.semaphore("D"))
        sb = lambda n, sh, dt: st.enter_context(nc.sbuf_tensor(n, sh, dt))
        TRI = sb("TRI", [128, 128], F32)
        IDF = sb("IDF", [128, 128], F32)
        IDB = sb("IDB", [128, 128], BF16)
        MS = sb("MS", [128, 128], F32)
        MI = sb("MI", [128, 128], F32)
        ONR = sb("ONR", [1, 128], F32)
        ONC = sb("ONC", [128, 1], F32)
        OH = sb("OH", [128, 1], F32)
        S0 = sb("S0", [128, 512], F32)
        S0b = sb("S0b", [128, 512], BF16)
        Kf = sb("Kf", [128, 128], F32)
        Qf = sb("Qf", [128, 128], F32)
        Kt = sb("Kt", [128, 128], F32)
        Vt = sb("Vt", [128, 128], F32)
        SQ = sb("SQ", [128, 128], F32)
        Knf = sb("Knf", [128, 128], BF16)
        Qnf = sb("Qnf", [128, 128], BF16)
        Knt = sb("Knt", [128, 128], BF16)
        nrm = sb("nrm", [1, 128], F32)
        rin = sb("rin", [1, 128], F32)
        rc = sb("rc", [128, 1], F32)
        lgS = sb("lgS", [128, 1], F32)
        btR = sb("btR", [1, 128], F32)
        btC = sb("btC", [128, 1], F32)
        cl = sb("cl", [128, 1], F32)
        Ac = sb("Ac", [128, 1], F32)
        clr = sb("clr", [1, 128], F32)
        ncl = sb("ncl", [1, 128], F32)
        clC = sb("clC", [128, 1], F32)
        c11 = sb("c11", [1, 1], F32)
        wl = sb("wl", [128, 1], F32)
        wc_ = sb("wc_", [128, 1], F32)
        ACc = sb("ACc", [128, 1], F32)
        FS = sb("FS", [128, 128], F32)
        FI = sb("FI", [128, 128], F32)
        GT = sb("GT", [128, 128], F32)
        Lts = sb("Lts", [128, 7, 128], BF16)
        Ls = sb("Ls", [128, 7, 128], BF16)
        LT0 = sb("LT0", [128, 128], F32)
        X = sb("X", [128, 128], BF16)
        Bx = sb("Bx", [128, 128], F32)
        EQ = sb("EQ", [128, 128], BF16)
        o1 = sb("o1", [128, 128], F32)
        Ob = sb("Ob", [128, 128], F32)
        Kw = sb("Kw", [128, 128], BF16)
        St = sb("St", [128, 128], F32)
        ps1 = st.enter_context(nc.psum_tensor("ps1", [128, 128], F32))
        ps2 = st.enter_context(nc.psum_tensor("ps2", [128, 128], F32))
        psb = st.enter_context(nc.psum_tensor("psb", [128, 128], BF16))
        psR = st.enter_context(nc.psum_tensor("psR", [1, 128], F32))
        psC = st.enter_context(nc.psum_tensor("psC", [128, 1], F32))
        ps11 = st.enter_context(nc.psum_tensor("ps11", [1, 1], F32))
        E = Em()
        for src, dst in [(triu, TRI), (idf, IDF), (idb, IDB), (mS, MS), (mI, MI), (onr, ONR), (onc, ONC), (oh, OH)]:
            E.d(lambda e, s=src, d=dst: e.dma_start(out=d[:], in_=s[:, :]))
        for h in range(4):
            E.d(lambda e, h=h: e.dma_start(out=S0[:, h * 128:h * 128 + 128], in_=Sin[h]))
        E.c('vector', lambda e: e.tensor_copy(S0b[:], S0[:]))
        for h in range(4):
            s0 = S0[:, h * 128:h * 128 + 128]
            s0b = S0b[:, h * 128:h * 128 + 128]
            for ch in range(8):
                t0 = ch * 128
                E.d(lambda e, h=h, t0=t0: e.dma_start(out=Kf[:], in_=kch[h, :, t0:t0 + 128]))
                E.d(lambda e, h=h, t0=t0: e.dma_start(out=Qf[:], in_=qch[h, :, t0:t0 + 128]))
                E.d(lambda e, h=h, ch=ch: e.dma_start(out=Kt[:], in_=ktk[h, ch]))
                E.d(lambda e, h=h, ch=ch: e.dma_start(out=Vt[:], in_=vtk[h, ch]))
                E.d(lambda e, h=h, ch=ch: e.dma_start(out=lgS[:], in_=lgc[h, ch]))
                E.d(lambda e, h=h, ch=ch: e.dma_start(out=btR[:], in_=btr[h, ch]))
                E.d(lambda e, h=h, ch=ch: e.dma_start(out=btC[:], in_=btc[h, ch]))
                E.c('scalar', lambda e: e.activation(SQ[:], Kf[:], AF.Square))
                E.c('tensor', lambda e: e.matmul(psR[:], ONC[:], SQ[:], start=True, stop=True))
                E.c('scalar', lambda e: e.activation(nrm[:], psR[:], AF.Sqrt))
                E.c('vector', lambda e: e.tensor_scalar(nrm[:], nrm[:], scalar1=1e-12, scalar2=1.0, op0=ALU.max, op1=ALU.mult))
                E.c('vector', lambda e: e.reciprocal(rin[:], nrm[:]))
                E.c('tensor', lambda e: e.matmul(ps1[:], ONR[:], rin[:], start=True, stop=True))
                E.c('vector', lambda e: e.tensor_mul(Knf[:], Kf[:], ps1[:]))
                E.c('tensor', lambda e: e.matmul(psC[:], rin[:], ONR[:, 0:1], start=True, stop=True))
                E.c('vector', lambda e: e.tensor_copy(rc[:], psC[:]))
                E.c('vector', lambda e: e.tensor_scalar_mul(Knt[:], Kt[:], rc[:]))
                E.c('scalar', lambda e: e.activation(SQ[:], Qf[:], AF.Square))
                E.c('tensor', lambda e: e.matmul(psR[:], ONC[:], SQ[:], start=True, stop=True))
                E.c('scalar', lambda e: e.activation(nrm[:], psR[:], AF.Sqrt))
                E.c('vector', lambda e: e.tensor_scalar(nrm[:], nrm[:], scalar1=1e-12, scalar2=1.0, op0=ALU.max, op1=ALU.mult))
                E.c('vector', lambda e: e.reciprocal(rin[:], nrm[:]))
                E.c('tensor', lambda e: e.matmul(ps1[:], ONR[:], rin[:], start=True, stop=True))
                E.c('vector', lambda e: e.tensor_mul(Qnf[:], Qf[:], ps1[:]))
                E.c('tensor', lambda e: e.matmul(psC[:], TRI[:], lgS[:], start=True, stop=True))
                E.c('vector', lambda e: e.tensor_copy(cl[:], psC[:]))
                E.c('scalar', lambda e: e.activation(Ac[:], cl[:], AF.Exp))
                E.c('tensor', lambda e: e.transpose(psR[:], cl[:], IDF[:]))
                E.c('vector', lambda e: e.tensor_copy(clr[:], psR[:]))
                E.c('vector', lambda e: e.tensor_scalar_mul(ncl[:], clr[:], -1.0))
                E.c('tensor', lambda e: e.matmul(ps2[:], ONR[:], clr[:], start=True, stop=False))
                E.c('tensor', lambda e: e.matmul(ps2[:], ncl[:], ONR[:], start=False, stop=True))
                E.c('vector', lambda e: e.tensor_sub(GT[:], ps2[:], MS[:]))
                E.c('scalar', lambda e: e.activation(FS[:], GT[:], AF.Exp))
                E.c('vector', lambda e: e.tensor_sub(GT[:], ps2[:], MI[:]))
                E.c('scalar', lambda e: e.activation(FI[:], GT[:], AF.Exp))
                E.c('tensor', lambda e: e.matmul(ps1[:], ONR[:], btR[:], start=True, stop=True))
                E.c('tensor', lambda e: e.matmul(ps2[:], Knf[:], Knf[:], start=True, stop=True))
                E.c('vector', lambda e: e.tensor_mul(LT0[:], ps2[:], FS[:]))
                E.c('vector', lambda e: e.tensor_mul(Lts[:, 0, :], LT0[:], ps1[:]))
                E.c('tensor', lambda e: e.transpose(psb[:], Lts[:, 0, :], IDB[:]))
                E.c('vector', lambda e: e.tensor_copy(Ls[:, 0, :], psb[:]))
                for lv in range(1, 7):
                    E.c('tensor', lambda e, lv=lv: e.matmul(ps2[:], Lts[:, lv - 1, :], Ls[:, lv - 1, :], start=True, stop=True))
                    E.c('vector', lambda e, lv=lv: e.tensor_copy(Ls[:, lv, :], ps2[:]))
                    E.c('tensor', lambda e, lv=lv: e.transpose(psb[:], Ls[:, lv, :], IDB[:]))
                    E.c('vector', lambda e, lv=lv: e.tensor_copy(Lts[:, lv, :], psb[:]))
                E.c('tensor', lambda e, s0b=s0b: e.matmul(ps2[:], Knf[:], s0b, start=True, stop=True))
                E.c('vector', lambda e: e.tensor_scalar_mul(Bx[:], ps2[:], Ac[:]))
                E.c('vector', lambda e: e.tensor_sub(Bx[:], Vt[:], Bx[:]))
                E.c('vector', lambda e: e.tensor_scalar(X[:], Bx[:], scalar1=btC[:], scalar2=1.0, op0=ALU.mult, op1=ALU.mult))
                for lv in range(6, 0, -1):
                    E.c('tensor', lambda e, lv=lv: e.matmul(ps2[:], Lts[:, lv, :], X[:], start=True, stop=True))
                    E.c('vector', lambda e: e.tensor_add(X[:], X[:], ps2[:]))
                E.c('tensor', lambda e: e.matmul(ps2[:], Lts[:, 0, :], X[:], start=True, stop=True))
                E.c('vector', lambda e: e.tensor_sub(X[:], X[:], ps2[:]))
                E.c('tensor', lambda e, s0b=s0b: e.matmul(ps1[:], Qnf[:], s0b, start=True, stop=True))
                E.c('vector', lambda e: e.tensor_scalar_mul(o1[:], ps1[:], Ac[:]))
                E.c('tensor', lambda e: e.matmul(ps2[:], Knf[:], Qnf[:], start=True, stop=True))
                E.c('vector', lambda e: e.tensor_mul(EQ[:], ps2[:], FI[:]))
                E.c('tensor', lambda e: e.matmul(ps1[:], EQ[:], X[:], start=True, stop=True))
                E.c('vector', lambda e: e.tensor_add(Ob[:], o1[:], ps1[:]))
                E.d(lambda e, h=h, ch=ch: e.dma_start(out=oo[h, ch], in_=Ob[:]))
                E.c('tensor', lambda e: e.matmul(ps11[:], OH[:], cl[:], start=True, stop=True))
                E.c('vector', lambda e: e.tensor_copy(c11[:], ps11[:]))
                E.c('tensor', lambda e: e.matmul(psC[:], ONR[:], c11[:], start=True, stop=True))
                E.c('vector', lambda e: e.tensor_copy(clC[:], psC[:]))
                E.c('vector', lambda e: e.tensor_sub(wl[:], clC[:], cl[:]))
                E.c('scalar', lambda e: e.activation(wc_[:], wl[:], AF.Exp))
                E.c('scalar', lambda e: e.activation(ACc[:], clC[:], AF.Exp))
                E.c('vector', lambda e: e.tensor_scalar_mul(Kw[:], Knt[:], wc_[:]))
                E.c('tensor', lambda e: e.matmul(ps2[:], Kw[:], X[:], start=True, stop=True))
                E.c('vector', lambda e, s0=s0: e.tensor_scalar_mul(St[:], s0, ACc[:]))
                E.c('vector', lambda e: e.tensor_add(St[:], St[:], ps2[:]))
                E.c('vector', lambda e, s0=s0: e.tensor_copy(s0, St[:]))
                E.c('vector', lambda e, s0b=s0b: e.tensor_copy(s0b, St[:]))
            E.d(lambda e, h=h: e.dma_start(out=So[h], in_=S0[:, h * 128:h * 128 + 128]))
        E.finalize(nc, C, D)
    return nc


def build_rmsgate():
    nc = _bass()
    o = nc.dram_tensor("o", [2048, 2048], F32, kind="ExternalInput").ap()
    xg = nc.dram_tensor("xg", [2048, 2048], F32, kind="ExternalInput").ap()
    nw = nc.dram_tensor("nw", [1, 2048], F32, kind="ExternalInput").ap()
    y = nc.dram_tensor("y", [2048, 2048], F32, kind="ExternalOutput").ap()
    with ExitStack() as st:
        C = st.enter_context(nc.semaphore("C"))
        D = st.enter_context(nc.semaphore("D"))
        ot = st.enter_context(nc.sbuf_tensor("ot", [128, 2048], F32))
        gt = st.enter_context(nc.sbuf_tensor("gt", [128, 2048], F32))
        sq = st.enter_context(nc.sbuf_tensor("sq", [128, 2048], F32))
        nwr = st.enter_context(nc.sbuf_tensor("nwr", [1, 2048], F32))
        nwb = st.enter_context(nc.sbuf_tensor("nwb", [128, 2048], F32))
        onr = st.enter_context(nc.sbuf_tensor("onr", [1, 128], F32))
        ss = st.enter_context(nc.sbuf_tensor("ss", [128, 1], F32))
        ps = st.enter_context(nc.psum_tensor("ps", [128, 512], F32))
        E = Em()
        E.d(lambda e: e.dma_start(out=nwr[:], in_=nw[:, :]))
        E.c('vector', lambda e: e.tensor_scalar(onr[:], nwr[:, 0:128], scalar1=0.0, scalar2=1.0, op0=ALU.mult, op1=ALU.add))
        for j in range(4):
            E.c('tensor', lambda e, j=j: e.matmul(ps[:], onr[:], nwr[:, j * 512:j * 512 + 512], start=True, stop=True))
            E.c('vector', lambda e, j=j: e.tensor_copy(nwb[:, j * 512:j * 512 + 512], ps[:]))
        for t in range(16):
            r0 = t * 128
            E.d(lambda e, r0=r0: e.dma_start(out=ot[:], in_=o[r0:r0 + 128, :]))
            E.d(lambda e, r0=r0: e.dma_start(out=gt[:], in_=xg[r0:r0 + 128, :]))
            E.c('scalar', lambda e: e.activation(sq[:], ot[:], AF.Square))
            E.c('vector', lambda e: e.reduce_sum(ss[:], sq[:], axis=AX.X))
            E.c('vector', lambda e: e.tensor_scalar(ss[:], ss[:], scalar1=1.0 / 2048.0, scalar2=1e-6, op0=ALU.mult, op1=ALU.add))
            E.c('scalar', lambda e: e.activation(ss[:], ss[:], AF.Ln))
            E.c('vector', lambda e: e.tensor_scalar_mul(ss[:], ss[:], -0.5))
            E.c('scalar', lambda e: e.activation(ss[:], ss[:], AF.Exp))
            E.c('vector', lambda e: e.tensor_scalar_mul(ot[:], ot[:], ss[:]))
            E.c('vector', lambda e: e.tensor_mul(ot[:], ot[:], nwb[:]))
            E.c('scalar', lambda e: e.activation(gt[:], gt[:], AF.Silu))
            E.c('vector', lambda e: e.tensor_mul(ot[:], ot[:], gt[:]))
            E.d(lambda e, r0=r0: e.dma_start(out=y[r0:r0 + 128, :], in_=ot[:]))
        E.finalize(nc, C, D)
    return nc


def build_glu():
    nc = _bass()
    g = nc.dram_tensor("g", [5632, 2048], F32, kind="ExternalInput").ap()
    u = nc.dram_tensor("u", [5632, 2048], F32, kind="ExternalInput").ap()
    z = nc.dram_tensor("z", [5632, 2048], F32, kind="ExternalOutput").ap()
    with ExitStack() as st:
        C = st.enter_context(nc.semaphore("C"))
        D = st.enter_context(nc.semaphore("D"))
        gt = st.enter_context(nc.sbuf_tensor("gt", [128, 2048], F32))
        ut = st.enter_context(nc.sbuf_tensor("ut", [128, 2048], F32))
        E = Em()
        for t in range(44):
            r0 = t * 128
            E.d(lambda e, r0=r0: e.dma_start(out=gt[:], in_=g[r0:r0 + 128, :]))
            E.d(lambda e, r0=r0: e.dma_start(out=ut[:], in_=u[r0:r0 + 128, :]))
            E.c('scalar', lambda e: e.activation(gt[:], gt[:], AF.Silu))
            E.c('vector', lambda e: e.tensor_mul(gt[:], gt[:], ut[:]))
            E.d(lambda e, r0=r0: e.dma_start(out=z[r0:r0 + 128, :], in_=gt[:]))
        E.finalize(nc, C, D)
    return nc


def build_ew():
    nc = _bass()
    x = nc.dram_tensor("x", [2048, 2048], F32, kind="ExternalInput").ap()
    s = nc.dram_tensor("s", [2048, 2048], F32, kind="ExternalInput").ap()
    ga = nc.dram_tensor("ga", [2048, 2048], F32, kind="ExternalInput").ap()
    gb = nc.dram_tensor("gb", [2048, 2048], F32, kind="ExternalInput").ap()
    y = nc.dram_tensor("y", [2048, 2048], F32, kind="ExternalOutput").ap()
    with ExitStack() as st:
        C = st.enter_context(nc.semaphore("C"))
        D = st.enter_context(nc.semaphore("D"))
        xt = st.enter_context(nc.sbuf_tensor("xt", [128, 2048], F32))
        st_ = st.enter_context(nc.sbuf_tensor("st_", [128, 2048], F32))
        at = st.enter_context(nc.sbuf_tensor("at", [128, 2048], F32))
        bt = st.enter_context(nc.sbuf_tensor("bt", [128, 2048], F32))
        E = Em()
        for t in range(16):
            r0 = t * 128
            E.d(lambda e, r0=r0: e.dma_start(out=xt[:], in_=x[r0:r0 + 128, :]))
            E.d(lambda e, r0=r0: e.dma_start(out=st_[:], in_=s[r0:r0 + 128, :]))
            E.d(lambda e, r0=r0: e.dma_start(out=at[:], in_=ga[r0:r0 + 128, :]))
            E.d(lambda e, r0=r0: e.dma_start(out=bt[:], in_=gb[r0:r0 + 128, :]))
            E.c('vector', lambda e: e.tensor_add(at[:], at[:], bt[:]))
            E.c('scalar', lambda e: e.activation(at[:], at[:], AF.Silu))
            E.c('vector', lambda e: e.tensor_mul(at[:], at[:], st_[:]))
            E.c('vector', lambda e: e.tensor_add(at[:], at[:], xt[:]))
            E.d(lambda e, r0=r0: e.dma_start(out=y[r0:r0 + 128, :], in_=at[:]))
        E.finalize(nc, C, D)
    return nc


_progs = {}


def _get(key, fn):
    if key not in _progs:
        _progs[key] = fn()
    return _progs[key]


def _run(nc, maps):
    res = run_bass_kernel_spmd(nc, maps, core_ids=list(range(NCORE)))
    return res.results


def _ac(a):
    return np.ascontiguousarray(a, dtype=np.float32)


def kernel(L0_attn_x, L0_attn_Wq, L0_attn_Wk, L0_attn_Wv, L0_attn_Wa, L0_attn_Wb,
           L0_attn_Wg, L0_attn_Wo, L0_attn_cw_q, L0_attn_cb_q, L0_attn_cw_k, L0_attn_cb_k,
           L0_attn_cw_v, L0_attn_cb_v, L0_attn_A_log, L0_attn_dt_bias, L0_attn_norm_w,
           L0_ffn_wq_gate, L0_ffn_s_gate, L0_ffn_wq_up, L0_ffn_s_up, L0_ffn_wq_down, L0_ffn_s_down):
    x = np.asarray(L0_attn_x, dtype=np.float32)
    xf = x.reshape(16384, 2048)
    R = 2048

    ln = _get('ln', build_ln)
    res = _run(ln, [{"x": xf[c * R:(c + 1) * R]} for c in range(NCORE)])
    xn = np.concatenate([r["y"] for r in res], axis=0)

    Wst = np.concatenate([np.asarray(w, np.float32) for w in
                          (L0_attn_Wq, L0_attn_Wk, L0_attn_Wv, L0_attn_Wg, L0_attn_Wa, L0_attn_Wb)], axis=0)
    WT = _ac(Wst.T)
    mm = _get('mm8224', lambda: build_mm(8224, 'p'))
    res = _run(mm, [{"xT": _ac(xn[c * R:(c + 1) * R].T), "WT": WT} for c in range(NCORE)])
    y = np.concatenate([r["y"] for r in res], axis=0)

    def chmajor(cols):
        a = y[:, cols].reshape(2, 8192, 2048).transpose(2, 0, 1)
        return np.pad(a, ((0, 0), (0, 0), (3, 0)))
    xp = np.concatenate([chmajor(slice(0, 2048)), chmajor(slice(2048, 4096)), chmajor(slice(4096, 6144))], axis=0)
    cwS = np.concatenate([np.asarray(w, np.float32) for w in (L0_attn_cw_q, L0_attn_cw_k, L0_attn_cw_v)], axis=0)
    cwT = _ac(cwS.T)[:, :, None]
    cbS = np.concatenate([np.asarray(w, np.float32) for w in (L0_attn_cb_q, L0_attn_cb_k, L0_attn_cb_v)], axis=0)[:, None]
    conv = _get('conv', build_conv)
    res = _run(conv, [{"xp": _ac(xp[c * 768:(c + 1) * 768]), "cw": _ac(cwT[:, c * 768:(c + 1) * 768]),
                       "cb": _ac(cbS[c * 768:(c + 1) * 768])} for c in range(NCORE)])
    qkv = np.concatenate([r["y"] for r in res], axis=0)
    qc, kc, vc = qkv[0:2048], qkv[2048:4096], qkv[4096:6144]

    aT = _ac(y[:, 8192:8208].T)
    bT = _ac(y[:, 8208:8224].T)
    dec = _get('dec', build_decay)
    Al = np.asarray(L0_attn_A_log, np.float32)[:, None]
    dtb = np.asarray(L0_attn_dt_bias, np.float32)[:, None]
    res = _run(dec, [{"aT": _ac(aT[:, c * R:(c + 1) * R]), "bT": _ac(bT[:, c * R:(c + 1) * R]),
                      "Al": Al, "dtb": dtb} for c in range(NCORE)])
    lgf = np.concatenate([r["lg"] for r in res], axis=1).reshape(16, 2, 8192)
    btf = np.concatenate([r["bt"] for r in res], axis=1).reshape(16, 2, 8192)

    scan = _get('scan', build_scan)
    triu = np.triu(np.ones((128, 128), np.float32))
    idf = np.eye(128, dtype=np.float32)
    idb = np.eye(128).astype(BF)
    sidx = np.arange(128)
    mS = np.where(sidx[:, None] < sidx[None, :], 0.0, 1e9).astype(np.float32)
    mI = np.where(sidx[:, None] <= sidx[None, :], 0.0, 1e9).astype(np.float32)
    onr = np.ones((1, 128), np.float32)
    onc = np.ones((128, 1), np.float32)
    oh = np.zeros((128, 1), np.float32)
    oh[127, 0] = 1.0
    consts = {"triu": triu, "idf": idf, "idb": idb, "mS": mS, "mI": mI, "onr": onr, "onc": onc, "oh": oh}
    heads = [(c // 4, [4 * (c % 4) + i for i in range(4)]) for c in range(NCORE)]
    S = [np.zeros((4, 128, 128), np.float32) for _ in range(NCORE)]
    o_full = np.zeros((2, 8192, 16, 128), np.float32)
    ktok = kc.transpose(1, 2, 0)
    vtok = vc.transpose(1, 2, 0)
    for l in range(8):
        t0 = l * 1024
        maps = []
        for c in range(NCORE):
            b, hs = heads[c]
            qch = _ac(np.stack([qc[h * 128:h * 128 + 128, b, t0:t0 + 1024] for h in hs]))
            kch = _ac(np.stack([kc[h * 128:h * 128 + 128, b, t0:t0 + 1024] for h in hs]))
            ktk = _ac(np.stack([ktok[b, t0:t0 + 1024, h * 128:h * 128 + 128] for h in hs])).reshape(4, 8, 128, 128)
            vtk = _ac(np.stack([vtok[b, t0:t0 + 1024, h * 128:h * 128 + 128] for h in hs])).reshape(4, 8, 128, 128)
            lgc = _ac(np.stack([lgf[h, b, t0:t0 + 1024] for h in hs])).reshape(4, 8, 128, 1)
            btr = _ac(np.stack([btf[h, b, t0:t0 + 1024] for h in hs])).reshape(4, 8, 1, 128)
            btc = btr.reshape(4, 8, 128, 1)
            m = {"qch": qch, "kch": kch, "ktk": ktk, "vtk": vtk, "lgc": lgc,
                 "btr": btr, "btc": btc, "Sin": S[c]}
            m.update(consts)
            maps.append(m)
        res = _run(scan, maps)
        for c in range(NCORE):
            b, hs = heads[c]
            S[c] = np.ascontiguousarray(res[c]["So"])
            ooc = res[c]["oo"].reshape(4, 1024, 128)
            for i, h in enumerate(hs):
                o_full[b, t0:t0 + 1024, h] = ooc[i]
    o_flat = o_full.reshape(16384, 2048)

    rg = _get('rg', build_rmsgate)
    nwr = np.asarray(L0_attn_norm_w, np.float32)[None, :]
    res = _run(rg, [{"o": _ac(o_flat[c * R:(c + 1) * R]), "xg": _ac(y[c * R:(c + 1) * R, 6144:8192]),
                     "nw": nwr} for c in range(NCORE)])
    sp = np.concatenate([r["y"] for r in res], axis=0)

    mmo = _get('mm2048', lambda: build_mm(2048, 'o'))
    WoT = _ac(np.asarray(L0_attn_Wo, np.float32).T)
    res = _run(mmo, [{"xT": _ac(sp[c * R:(c + 1) * R].T), "WT": WoT} for c in range(NCORE)])
    attn = np.concatenate([r["y"] for r in res], axis=0)

    qmmgu = _get('qmmgu', lambda: build_qmm(2048, 5632, 'g'))
    WgT = np.ascontiguousarray(np.asarray(L0_ffn_wq_gate).T.astype(np.int32))
    WuT = np.ascontiguousarray(np.asarray(L0_ffn_wq_up).T.astype(np.int32))
    sg = np.asarray(L0_ffn_s_gate, np.float32)[:, None]
    su = np.asarray(L0_ffn_s_up, np.float32)[:, None]
    xnTs = [_ac(xn[c * R:(c + 1) * R].T) for c in range(NCORE)]
    res = _run(qmmgu, [{"xT": xnTs[c], "WT": WgT, "sc": sg} for c in range(NCORE)])
    gT = [np.ascontiguousarray(r["y"]) for r in res]
    res = _run(qmmgu, [{"xT": xnTs[c], "WT": WuT, "sc": su} for c in range(NCORE)])
    uT = [np.ascontiguousarray(r["y"]) for r in res]
    glu = _get('glu', build_glu)
    res = _run(glu, [{"g": gT[c], "u": uT[c]} for c in range(NCORE)])
    zT = [np.ascontiguousarray(r["z"]) for r in res]
    qmmd = _get('qmmd', lambda: build_qmm(2816, 2048, 'd'))
    WdT = np.ascontiguousarray(np.asarray(L0_ffn_wq_down).T.astype(np.int32))
    sd = np.asarray(L0_ffn_s_down, np.float32)[:, None]
    res = _run(qmmd, [{"xT": _ac(zT[c][:2816]), "WT": _ac_i(WdT[:2816]), "sc": sd} for c in range(NCORE)])
    gfa = [np.ascontiguousarray(r["y"].T) for r in res]
    res = _run(qmmd, [{"xT": _ac(zT[c][2816:]), "WT": _ac_i(WdT[2816:]), "sc": sd} for c in range(NCORE)])
    gfb = [np.ascontiguousarray(r["y"].T) for r in res]

    ew = _get('ew', build_ew)
    res = _run(ew, [{"x": xf[c * R:(c + 1) * R], "s": _ac(attn[c * R:(c + 1) * R]),
                     "ga": gfa[c], "gb": gfb[c]} for c in range(NCORE)])
    out = np.concatenate([r["y"] for r in res], axis=0)
    return out.reshape(2, 8192, 2048).astype(np.float32)


def _ac_i(a):
    return np.ascontiguousarray(a, dtype=np.int32)
